# version 29
# baseline (speedup 1.0000x reference)
import numpy as np
from contextlib import ExitStack

import concourse.bass as bass
import concourse.mybir as mybir
from concourse.bass_utils import run_bass_kernel_spmd

F32 = mybir.dt.float32
BF16 = mybir.dt.bfloat16
F32R = mybir.dt.float32r
AF = mybir.ActivationFunctionType
ALU = mybir.AluOpType
AX = mybir.AxisListType

NCORES = 8
T_OWN = 2048
HALO = 256
T_TOT = T_OWN + HALO
D = 1024
I = 512
E = 16
DS = 64
TB = 512
EPS = 1e-5

A_TILES = [(0, HALO)] + [(HALO + t * TB, TB) for t in range(4)]


class Sem:

    def __init__(self, handle):
        self.h = handle
        self.n = 0


class Inst:
    __slots__ = ("raw",)

    def __init__(self, raw):
        self.raw = raw

    def inc(self, sem, k=1):
        sem.n += k
        if self.raw is not None:
            self.raw.then_inc(sem.h, k)
        return sem.n


class Eng:

    def __init__(self, real):
        self.real = real
        self.hwm = {}

    def op(self, name, *a, **kw):
        if self.real is None:
            return Inst(None)
        return Inst(getattr(self.real, name)(*a, **kw))

    def wait(self, sem, val):
        key = id(sem.h)
        if self.hwm.get(key, -1) >= val:
            return
        self.hwm[key] = val
        if self.real is not None:
            self.real.wait_ge(sem.h, val)

    def drain(self):
        if self.real is not None:
            self.real.drain()


def build_nc():
    nc = bass.Bass("TRN2", target_bir_lowering=False)

    xt = nc.dram_tensor("xt", [D, T_TOT], F32, kind="ExternalInput").ap()
    adiag = nc.dram_tensor("adiag", [DS], F32, kind="ExternalInput").ap()
    bmat = nc.dram_tensor("bmat", [D, DS], F32, kind="ExternalInput").ap()
    cmat = nc.dram_tensor("cmat", [DS, D], F32, kind="ExternalInput").ap()
    dmat = nc.dram_tensor("dmat", [D, D], F32, kind="ExternalInput").ap()
    gw = nc.dram_tensor("gw", [D, E], F32, kind="ExternalInput").ap()
    gb = nc.dram_tensor("gb", [E], F32, kind="ExternalInput").ap()
    swg = nc.dram_tensor("swg", [D, I], F32, kind="ExternalInput").ap()
    swu = nc.dram_tensor("swu", [D, I], F32, kind="ExternalInput").ap()
    swd = nc.dram_tensor("swd", [I, D], F32, kind="ExternalInput").ap()
    ewg = nc.dram_tensor("ewg", [E, D, I], F32, kind="ExternalInput").ap()
    ewu = nc.dram_tensor("ewu", [E, D, I], F32, kind="ExternalInput").ap()
    ewd = nc.dram_tensor("ewd", [E, I, D], F32, kind="ExternalInput").ap()
    ident_in = nc.dram_tensor("ident_in", [128, 128], F32, kind="ExternalInput").ap()
    selm_in = nc.dram_tensor("selm_in", [E, E * 128], F32, kind="ExternalInput").ap()
    ones_in = nc.dram_tensor("ones_in", [128, 1], F32, kind="ExternalInput").ap()
    onesr_in = nc.dram_tensor("onesr_in", [1, 128], F32, kind="ExternalInput").ap()
    out = nc.dram_tensor("out", [D, T_OWN], F32, kind="ExternalOutput").ap()

    es = ExitStack()
    with es:
        sb = lambda n, s, d: es.enter_context(nc.sbuf_tensor(n, s, d))
        ps = lambda n, s: es.enter_context(nc.psum_tensor(n, s, F32))

        xt_buf = sb("xt_buf", [128, 8 * TB], F32)
        acc = sb("acc", [128, 8 * T_OWN], F32)
        xnT = sb("xnT", [128, 8 * T_TOT], BF16)
        warena = sb("warena", [128, 5 * 4096], BF16)
        csb = sb("csb", [DS, D], BF16)
        bsb = sb("bsb", [128, 8 * DS], BF16)
        gsb = sb("gsb", [128, 8 * E], BF16)
        gb_sb = sb("gb_sb", [E, 1], F32)
        m_em = sb("m_em", [E, T_OWN], BF16)
        hs = sb("hs", [DS, T_TOT], BF16)
        rA = sb("rA", [1, TB], F32)
        rB = sb("rB", [1, TB], F32)
        rC = sb("rC", [1, TB], F32)
        rD = sb("rD", [1, TB], F32)
        rDb = rD[:, :].bitcast(BF16)
        rCb = rDb[0:1, 0:TB]
        rAb = rDb[0:1, TB:2 * TB]
        x2 = sb("x2", [128, 4 * TB], BF16)
        xb = sb("xb", [128, 2 * TB], BF16)
        rtb4 = sb("rtb4", [128, 64], F32)
        mk1 = sb("mk1", [128, 64], F32)
        rr2 = sb("rr2", [128, 64], F32)
        mk2 = sb("mk2", [128, 64], F32)
        m_tm = sb("m_tm", [128, 64], F32)
        v1t = [sb(f"v1t{b}", [128, 1], F32) for b in range(4)]
        v2t = [sb(f"v2t{b}", [128, 1], F32) for b in range(4)]
        dent = [sb(f"dent{b}", [128, 1], F32) for b in range(4)]
        rdent = [sb(f"rdent{b}", [128, 1], F32) for b in range(4)]
        sg1 = [sb(f"sg1{b}", [128, 1], F32) for b in range(4)]
        sg2 = [sb(f"sg2{b}", [128, 1], F32) for b in range(4)]
        a_sb = sb("a_sb", [DS, 1], F32)
        at1 = sb("at1", [DS, 1], F32)
        at2 = sb("at2", [DS, 1], F32)
        sig_a = sb("sig_a", [DS, 1], F32)
        silu_sb = sb("silu_sb", [128, 4 * TB], BF16)
        t1_sb = sb("t1_sb", [128, 4 * TB], BF16)
        inter_sb = sb("inter_sb", [128, 4 * TB], BF16)
        inter2_sb = sb("inter2_sb", [128, 4 * TB], BF16)
        m_sb = sb("m_sb", [128, TB], BF16)
        m2_sb = sb("m2_sb", [128, TB], BF16)
        g_sb = silu_sb[:, 0:2 * TB]
        out_buf = inter_sb[:, :].bitcast(F32)
        x2fv = x2[:, :].bitcast(F32)
        r_sb = x2fv[0:E, 0:TB]
        r2_sb = x2fv[0:E, TB:2 * TB]
        ident = sb("ident", [128, 128], F32)
        selm = sb("selm", [E, E * 128], BF16)
        onesc = sb("onesc", [128, 1], BF16)
        onescf = sb("onescf", [128, 1], F32)
        onesr = sb("onesr", [1, 128], F32)
        onesrb = sb("onesrb", [1, 128], BF16)

        pb = [ps(f"pb{i}", [128, 512]) for i in range(8)]

        sem_names = ["s_x", "s_x2", "s_xb", "s_xb2", "s_o", "s_o1", "s_gc",
                     "s_w0", "s_w1", "s_w2", "s_w3",
                     "s_pe", "s_v", "s_sc", "s_gp"]
        handles = {n: es.enter_context(nc.semaphore(n)) for n in sem_names}

        block = es.enter_context(nc.Block())

        def xt_c(c, w=TB):
            return xt_buf[:, c * TB:c * TB + w]

        def acc_v(c, tb, w=TB, off=0):
            return acc[:, c * T_OWN + tb * TB + off:c * T_OWN + tb * TB + off + w]

        def xnT_c(c, off, w):
            return xnT[:, c * T_TOT + off:c * T_TOT + off + w]

        def wslot(slot, k, oc, kind):
            base = slot * 4096
            if kind == "gu":
                o = base + k * 512 + oc * 128
            else:
                o = base + k * 1024 + oc * 128
            return warena[:, o:o + 128]

        def dview(k, oc):
            o = (1 + k // 4) * 4096 + (k % 4) * 1024 + oc * 128
            return warena[:, o:o + 128]

        def emit(active):
            S = {n: Sem(handles[n]) for n in sem_names}
            sx, sx2, sgc = S["s_x"], S["s_x2"], S["s_gc"]
            sxb, sxb2 = S["s_xb"], S["s_xb2"]
            so_sems = [S["s_o"], S["s_o1"]]
            sw = [S["s_w0"], S["s_w1"], S["s_w2"], S["s_w3"]]
            spe, sv, ssc, sgp = S["s_pe"], S["s_v"], S["s_sc"], S["s_gp"]

            EN = {n: Eng(active.get(n)) for n in
                  ["sync", "gpsimd", "tensor", "scalar", "vector"]}
            SY, G, T, SC, V = (EN["sync"], EN["gpsimd"], EN["tensor"],
                               EN["scalar"], EN["vector"])

            lastv = [0] * 8
            lastsc = [0] * 8

            def t_wait_bank(i):
                if lastv[i]:
                    T.wait(sv, lastv[i])
                if lastsc[i]:
                    T.wait(ssc, lastsc[i])

            SY.op("dma_start", out=ident[:, :], in_=ident_in).inc(sx, 16)
            SY.op("dma_start", out=onesr[:, :], in_=onesr_in).inc(sx, 16)
            SY.op("dma_start", out=onescf[:, :], in_=ones_in).inc(sx, 16)
            SY.op("dma_start", out=out, in_=xt[:, HALO:T_TOT]).inc(sx, 16)
            n_sx = 4

            G.op("dma_start", out=selm[:, :], in_=selm_in).inc(sgc, 16)
            G.op("dma_start", out=onesc[:, :], in_=ones_in).inc(sgc, 16)
            G.op("dma_start", out=onesrb[:, :], in_=onesr_in).inc(sgc, 16)
            G.op("dma_start", out=a_sb[:, :], in_=adiag[:, None]).inc(sgc, 16)
            G.op("dma_start", out=gb_sb[:, :], in_=gb[:, None]).inc(sgc, 16)
            G.op("dma_start", out=bsb[:, :].rearrange("p (c s) -> p c s", c=8),
                 in_=bmat.rearrange("(c p) s -> p c s", p=128)).inc(sgc, 16)
            G.op("dma_start", out=csb[:, :], in_=cmat).inc(sgc, 16)
            G.op("dma_start", out=gsb[:, :].rearrange("p (c e) -> p c e", c=8),
                 in_=gw.rearrange("(c p) e -> p c e", p=128)).inc(sgc, 16)
            n_gc = 8

            RETIRE = {}

            def w_src(m):
                e, j = divmod(m, 3)
                if e == 0:
                    hb = [swg, swu, swd][j]
                else:
                    hb = [ewg[e - 1], ewu[e - 1], ewd[e - 1]][j]
                if j < 2:
                    return hb.rearrange("(c p) i -> p c i", p=128), "gu"
                return hb.rearrange("(c p) d -> p c d", p=128), "d"

            slot_uses = [0, 0, 0, 0]

            SC.wait(sgc, 16 * n_gc)
            k = SC.op("activation", at1[:, :], a_sb[:, :], AF.Exp, scale=-1.0).inc(ssc, 1)
            V.wait(ssc, k)
            V.op("tensor_scalar_add", at2[:, :], at1[:, :], 1.0)
            V.drain()
            SIG_DONE = V.op("reciprocal", sig_a[:, :], at2[:, :]).inc(sv, 1)

            def gdma(m, wait_tkt):
                if wait_tkt:
                    G.wait(spe, wait_tkt)
                src, kind = w_src(m)
                slot = m % 4
                if kind == "gu":
                    dst = warena[:, slot * 4096:(slot + 1) * 4096].rearrange(
                        "p (c i) -> p c i", c=8)
                else:
                    dst = warena[:, slot * 4096:(slot + 1) * 4096].rearrange(
                        "p (c d) -> p c d", c=4)
                G.op("dma_start", out=dst, in_=src).inc(sw[slot], 16)
                slot_uses[slot] += 1

            for m in range(4):
                gdma(m, None)

            CAST_T = {}
            cast_last = [0]
            PEND = []

            def wstage_dma(m):
                nonlocal n_sx
                src, kind = w_src(m)
                nch = 8 if kind == "gu" else 4
                if cast_last[0]:
                    SY.wait(sv, cast_last[0])
                else:
                    SY.wait(sv, PHASE_A_V)
                SY.op("dma_start",
                      out=xt_buf[:, :].rearrange("p (c t) -> p c t", c=nch),
                      in_=src).inc(sx, 16)
                n_sx += 1
                PEND.append((m, kind, m % 5, 16 * n_sx))

            def wstage_cast():
                m, kind, slot, sxt = PEND.pop(0)
                nch, wch = (8, 512) if kind == "gu" else (4, 1024)
                V.wait(sx, sxt)
                r = RETIRE.get(m - 5)
                if r:
                    V.wait(spe, r)
                for c in range(nch):
                    kk = V.op("tensor_copy",
                              warena[:, slot * 4096 + c * wch:
                                     slot * 4096 + (c + 1) * wch],
                              xt_buf[:, c * wch:(c + 1) * wch])
                CAST_T[m] = kk.inc(sv, 1)
                cast_last[0] = CAST_T[m]

            def wstage(m, retire_tkt, src=None, kind=None, slot=None):
                nonlocal n_sx
                if src is None:
                    src, kind = w_src(m)
                    slot = m % 5
                nch, wch = (8, 512) if kind == "gu" else (4, 1024)
                if cast_last[0]:
                    SY.wait(sv, cast_last[0])
                else:
                    SY.wait(sv, PHASE_A_V)
                SY.op("dma_start",
                      out=xt_buf[:, :].rearrange("p (c t) -> p c t", c=nch),
                      in_=src).inc(sx, 16)
                n_sx += 1
                V.wait(sx, 16 * n_sx)
                if retire_tkt:
                    V.wait(spe, retire_tkt)
                for c in range(nch):
                    kk = V.op("tensor_copy",
                              warena[:, slot * 4096 + c * wch:
                                     slot * 4096 + (c + 1) * wch],
                              xt_buf[:, c * wch:(c + 1) * wch])
                CAST_T[m] = kk.inc(sv, 1)
                cast_last[0] = CAST_T[m]

            XNT_RDY = [0] * 5
            XT_FREE = [0] * 5
            XT_FREE_G = [0] * 5
            SCAN_T = [0] * 5
            M_RDY = [0] * 4
            stats_mm_t = {}
            n_sx2 = [0]
            n_sxb = [0]
            n_sxb2 = [0]
            prev_gacc = 0
            prev_rt_consumed = 0
            prev_topk = 0
            prev_mevac = 0
            prev_inter = {}
            ACC_T = {}
            steps = [(e, tb) for e in range(E + 1) for tb in range(4)]
            inter_bufs = [inter_sb, inter2_sb]
            m_bufs = [m_sb, m2_sb]
            PHASE_A_V = None

            EXP_T = {}
            xt2_f = [silu_sb[:, :].bitcast(F32), t1_sb[:, :].bitcast(F32),
                     inter_sb[:, :].bitcast(F32), inter2_sb[:, :].bitcast(F32)]

            def xt_v(it, c, w=TB):
                if it % 2 == 0:
                    return xt_buf[:, c * TB:c * TB + w]
                return xt2_f[c // 2][:, (c % 2) * TB:(c % 2) * TB + w]

            def emit_tile_front(it):
                nonlocal n_sx
                off, W = A_TILES[it]
                sb0, sb1 = (4, 5) if it % 2 == 0 else (0, 1)
                if it >= 2:
                    SY.wait(sv, XT_FREE[it - 2])
                xsrc = xt.rearrange("(c p) t -> p c t", p=128)
                if it % 2 == 0:
                    SY.op("dma_start",
                          out=xt_buf[:, :].rearrange("p (c t) -> p c t", c=8)[:, 0:4, 0:W],
                          in_=xsrc[:, 0:4, off:off + W]).inc(sx, 16)
                    n_sx += 1
                    SY.op("dma_start",
                          out=xt_buf[:, :].rearrange("p (c t) -> p c t", c=8)[:, 4:8, 0:W],
                          in_=xsrc[:, 4:8, off:off + W]).inc(sx2, 16)
                    n_sx2[0] += 1
                else:
                    for q in range(4):
                        dst = xt2_f[q].rearrange("p (c t) -> p c t", c=2)[:, :, 0:W]
                        ins = SY.op("dma_start", out=dst,
                                    in_=xsrc[:, 2 * q:2 * q + 2, off:off + W])
                        ins.inc(sxb if q < 2 else sxb2, 16)
                    n_sxb[0] += 2
                    n_sxb2[0] += 2
                c0s, c0n = (sx, 16 * n_sx) if it % 2 == 0 else (sxb, 16 * n_sxb[0])
                c4s, c4n = (sx2, 16 * n_sx2[0]) if it % 2 == 0 else (sxb2, 16 * n_sxb2[0])
                for c in range(8):
                    if c == 0:
                        V.wait(c0s, c0n)
                    if c == 4:
                        V.wait(c4s, c4n)
                    if it >= 1 and c < 2:
                        V.wait(spe, stats_mm_t[(it - 1, c + 6)])
                    if c >= 2:
                        V.wait(spe, stats_mm_t[(it, c - 2)])
                    V.op("tensor_copy", xb[:, (c % 2) * TB:(c % 2) * TB + W],
                         xt_v(it, c, W))
                    k = V.op("tensor_tensor", x2[:, (c % 4) * TB:(c % 4) * TB + W],
                             xt_v(it, c, W), xt_v(it, c, W), ALU.mult).inc(sv, 1)
                    if c == 0:
                        if it == 0:
                            T.wait(sgc, 16 * n_gc)
                        T.wait(c0s, c0n)
                        t_wait_bank(sb0)
                        t_wait_bank(sb1)
                    T.wait(sv, k)
                    T.op("matmul", pb[sb0][0:1, 0:W], onesc[:, :],
                         xb[:, (c % 2) * TB:(c % 2) * TB + W],
                         start=(c == 0), stop=(c == 7))
                    kk = T.op("matmul", pb[sb1][0:1, 0:W], onesc[:, :],
                              x2[:, (c % 4) * TB:(c % 4) * TB + W],
                              start=(c == 0), stop=(c == 7)).inc(spe, 1)
                    stats_mm_t[(it, c)] = kk

            def emit_tile_mid(it):
                off, W = A_TILES[it]
                sb0, sb1 = (4, 5) if it % 2 == 0 else (0, 1)
                V.wait(spe, stats_mm_t[(it, 7)])
                V.op("tensor_copy", rA[:, 0:W], pb[sb0][0:1, 0:W])
                k = V.op("tensor_single_scalar", rB[:, 0:W], pb[sb1][0:1, 0:W],
                         float(D) * EPS, ALU.add).inc(sv, 1)
                lastv[sb0] = lastv[sb1] = k
                V.drain()
                V.op("tensor_scalar_mul", rC[:, 0:W], rA[:, 0:W], 1.0 / D)
                V.drain()
                V.op("tensor_mul", rD[:, 0:W], rC[:, 0:W], rC[:, 0:W])
                V.drain()
                k = V.op("scalar_tensor_tensor", rA[:, 0:W], rB[:, 0:W], 1.0 / D,
                         rD[:, 0:W], ALU.mult, ALU.subtract).inc(sv, 1)
                SC.wait(sv, k)
                SC.op("activation", rB[:, 0:W], rA[:, 0:W], AF.Ln)
                SC.drain()
                EXP_T[it] = SC.op("activation", rA[:, 0:W], rB[:, 0:W], AF.Exp,
                                  scale=-0.5).inc(ssc, 1)

            def emit_tile_back(it):
                nonlocal PHASE_A_V
                off, W = A_TILES[it]
                sb0, sb1 = (4, 5) if it % 2 == 0 else (0, 1)
                own = it >= 1
                tb = it - 1
                V.wait(ssc, EXP_T[it])
                V.op("tensor_copy", rCb[:, 0:W], rC[:, 0:W])
                k = V.op("tensor_copy", rAb[:, 0:W], rA[:, 0:W]).inc(sv, 1)
                T.wait(sv, k)
                t_wait_bank(sb0)
                t_wait_bank(sb1)
                T.op("matmul", pb[sb0][:, 0:W], onesrb[:, :], rCb[:, 0:W],
                     start=True, stop=True)
                k = T.op("matmul", pb[sb1][:, 0:W], onesrb[:, :], rAb[:, 0:W],
                         start=True, stop=True).inc(spe, 1)

                V.wait(spe, k)
                if own:
                    for c in range(8):
                        kk = V.op("tensor_tensor", acc_v(c, tb), xt_v(it, c),
                                  pb[sb0][:, 0:TB], ALU.subtract)
                    XT_FREE[it] = kk.inc(sv, 1)
                    V.drain()
                    for c in range(8):
                        V.op("tensor_tensor", acc_v(c, tb), acc_v(c, tb),
                             pb[sb1][:, 0:TB], ALU.mult)
                    V.drain()
                    V.op("tensor_copy",
                         xnT[:, :].rearrange("p (c t) -> p c t", c=8)[:, :, off:off + TB],
                         acc[:, :].rearrange("p (c t) -> p c t", c=8)[:, :, tb * TB:(tb + 1) * TB])
                else:
                    V.wait(spe, stats_mm_t[(1, 7)])
                    for r0 in (0, 4):
                        for c in range(r0, r0 + 4):
                            kk = V.op("tensor_tensor", x2[:, (c - r0) * W:(c - r0 + 1) * W],
                                 xt_v(it, c, W), pb[sb0][:, 0:W], ALU.subtract)
                        if r0 == 4:
                            XT_FREE[it] = kk.inc(sv, 1)
                        V.drain()
                        for c in range(r0, r0 + 4):
                            V.op("tensor_tensor", xnT_c(c, 0, W),
                                 x2[:, (c - r0) * W:(c - r0 + 1) * W],
                                 pb[sb1][:, 0:W], ALU.mult)
                        V.drain()
                XNT_RDY[it] = V.op("engine_nop").inc(sv, 1)
                lastv[sb0] = lastv[sb1] = XNT_RDY[it]

                T.wait(sv, XNT_RDY[it])
                t_wait_bank(7)
                for c in range(8):
                    mm = T.op("matmul", pb[7][0:DS, 0:W],
                              bsb[:, c * DS:(c + 1) * DS],
                              xnT_c(c, off, W),
                              start=(c == 0), stop=(c == 7))
                k = mm.inc(spe, 1)
                V.wait(spe, k)
                V.wait(sv, SIG_DONE)
                V.drain()
                init = 0.0 if it == 0 else hs[:, off - 1:off]
                SCAN_T[it] = V.op("tensor_tensor_scan", hs[:, off:off + W],
                                  sig_a[:, :].broadcast_to((DS, W)),
                                  pb[7][0:DS, 0:W], init,
                                  ALU.mult, ALU.add).inc(sv, 1)
                lastv[7] = SCAN_T[it]
                if it == 4:
                    PHASE_A_V = XT_FREE[4]

            def emit_tile_post(tb):
                nonlocal prev_topk, prev_mevac
                it = tb + 1
                off, W = A_TILES[it]
                if True:
                    T.wait(sv, XNT_RDY[it])
                    t_wait_bank(6)
                    for c in range(8):
                        mm = T.op("matmul", pb[6][0:E, 0:TB],
                                  gsb[:, c * E:(c + 1) * E],
                                  xnT_c(c, off, TB),
                                  start=(c == 0), stop=(c == 7))
                    k = mm.inc(spe, 1)
                    V.wait(spe, k)
                    k = V.op("tensor_scalar_add", r_sb, pb[6][0:E, 0:TB],
                             gb_sb[:, 0:1]).inc(sv, 1)
                    lastv[6] = k
                    T.wait(sv, k)
                    t_wait_bank(6)
                    for b in range(4):
                        mm = T.op("transpose", pb[6][:, b * 16:(b + 1) * 16],
                                  r_sb[:, b * 128:(b + 1) * 128],
                                  ident[0:E, 0:E])
                    k = mm.inc(spe, 1)
                    V.wait(spe, k)
                    for b in range(4):
                        V.op("tensor_copy", rtb4[:, b * 16:(b + 1) * 16],
                             pb[6][:, b * 16:(b + 1) * 16])
                    V.drain()
                    for b in range(4):
                        V.op("reduce_max", v1t[b][:, :],
                             rtb4[:, b * 16:(b + 1) * 16], axis=AX.X)
                    V.drain()
                    for b in range(4):
                        V.op("tensor_tensor", mk1[:, b * 16:(b + 1) * 16],
                             rtb4[:, b * 16:(b + 1) * 16],
                             v1t[b][:, :].broadcast_to((128, 16)), ALU.is_ge)
                    V.drain()
                    for b in range(4):
                        V.op("scalar_tensor_tensor", rr2[:, b * 16:(b + 1) * 16],
                             mk1[:, b * 16:(b + 1) * 16], -2.0,
                             rtb4[:, b * 16:(b + 1) * 16], ALU.mult, ALU.add)
                    V.drain()
                    for b in range(4):
                        V.op("reduce_max", v2t[b][:, :],
                             rr2[:, b * 16:(b + 1) * 16], axis=AX.X)
                    V.drain()
                    for b in range(4):
                        V.op("tensor_tensor", mk2[:, b * 16:(b + 1) * 16],
                             rtb4[:, b * 16:(b + 1) * 16],
                             v2t[b][:, :].broadcast_to((128, 16)), ALU.is_ge)
                    k = V.op("engine_nop").inc(sv, 1)
                    SC.wait(sv, k)
                    for b in range(4):
                        SC.op("activation", sg1[b][:, :], v1t[b][:, :], AF.Exp,
                              scale=-1.0)
                        kk = SC.op("activation", sg2[b][:, :], v2t[b][:, :],
                                   AF.Exp, scale=-1.0)
                    k = kk.inc(ssc, 1)
                    V.wait(ssc, k)
                    for b in range(4):
                        V.op("tensor_scalar_add", sg1[b][:, :], sg1[b][:, :], 1.0)
                        V.op("tensor_scalar_add", sg2[b][:, :], sg2[b][:, :], 1.0)
                    V.drain()
                    for b in range(4):
                        V.op("reciprocal", v1t[b][:, :], sg1[b][:, :])
                        V.op("reciprocal", v2t[b][:, :], sg2[b][:, :])
                    V.drain()
                    for b in range(4):
                        V.op("tensor_add", dent[b][:, :], v1t[b][:, :],
                             v2t[b][:, :])
                    V.drain()
                    for b in range(4):
                        V.op("reciprocal", rdent[b][:, :], dent[b][:, :])
                    V.drain()
                    for b in range(4):
                        V.op("tensor_mul", sg1[b][:, :], v1t[b][:, :],
                             rdent[b][:, :])
                        V.op("tensor_mul", sg2[b][:, :], v2t[b][:, :],
                             rdent[b][:, :])
                        V.op("tensor_sub", mk2[:, b * 16:(b + 1) * 16],
                             mk2[:, b * 16:(b + 1) * 16],
                             mk1[:, b * 16:(b + 1) * 16])
                    V.drain()
                    for b in range(4):
                        V.op("tensor_single_scalar", m_tm[:, b * 16:(b + 1) * 16],
                             mk1[:, b * 16:(b + 1) * 16], sg1[b][:, 0:1],
                             ALU.mult)
                        V.op("tensor_single_scalar", rr2[:, b * 16:(b + 1) * 16],
                             mk2[:, b * 16:(b + 1) * 16], sg2[b][:, 0:1],
                             ALU.mult)
                    V.drain()
                    k = V.op("tensor_add", m_tm[:, 0:64], m_tm[:, 0:64],
                             rr2[:, 0:64]).inc(sv, 1)
                    prev_topk = k
                    lastv[6] = k
                    T.wait(sv, k)
                    t_wait_bank(6)
                    for b in range(4):
                        mm = T.op("transpose", pb[6][0:E, b * 128:(b + 1) * 128],
                                  m_tm[:, b * 16:(b + 1) * 16],
                                  ident[:, :])
                    k = mm.inc(spe, 1)
                    V.wait(spe, k)
                    k = V.op("tensor_copy", m_em[:, tb * TB:(tb + 1) * TB],
                             pb[6][0:E, 0:TB]).inc(sv, 1)
                    M_RDY[tb] = k
                    lastv[6] = k

            def down_group(k2, half):
                e2, tb2 = steps[k2]
                ibuf2 = inter_bufs[k2 % 2]
                T.wait(sv, prev_inter[k2])
                if tb2 == 0 and half == 0:
                    m = 3 * e2 + 2
                    if m < 4:
                        T.wait(sw[m % 4], 16)
                    else:
                        T.wait(sv, CAST_T[m])
                for pr in (2 * half, 2 * half + 1):
                    banks = (4, 5) if pr % 2 == 0 else (6, 7)
                    for bank in banks:
                        t_wait_bank(bank)
                    for oc in (2 * pr, 2 * pr + 1):
                        bank = banks[oc % 2]
                        for kk4 in range(4):
                            kk = T.op("matmul", pb[bank][:, :],
                                      wslot((3 * e2 + 2) % 5, kk4, oc, "d"),
                                      ibuf2[:, kk4 * TB:(kk4 + 1) * TB],
                                      start=(kk4 == 0), stop=(kk4 == 3))
                    DOWN = kk.inc(spe, 1)
                    if tb2 == 3 and pr == 3:
                        RETIRE[3 * e2 + 2] = DOWN
                    V.wait(spe, DOWN)
                    for oc in (2 * pr, 2 * pr + 1):
                        bank = banks[oc % 2]
                        kk = V.op("tensor_tensor", acc_v(oc, tb2),
                                  acc_v(oc, tb2), pb[bank][:, :], ALU.add)
                    AT = kk.inc(sv, 1)
                    ACC_T[(e2, tb2, pr)] = AT
                    for bank in banks:
                        lastv[bank] = AT

            def emit_step(k):
                e, tb = steps[k]
                off = HALO + tb * TB
                mbuf = m_bufs[k % 2]
                ibuf = inter_bufs[k % 2]
                if PEND:
                    wstage_cast()
                if e > 0:
                    t_wait_bank(4)
                    T.wait(sv, M_RDY[tb])
                    kk = T.op("matmul", pb[4][:, :],
                              selm[:, (e - 1) * 128:e * 128],
                              m_em[:, tb * TB:(tb + 1) * TB],
                              start=True, stop=True).inc(spe, 1)
                    V.wait(spe, kk)
                    MB_T = V.op("tensor_copy", mbuf[:, :], pb[4][:, :]).inc(sv, 1)
                    lastv[4] = MB_T
                if tb == 0:
                    for j in range(2):
                        m = 3 * e + j
                        if m < 4:
                            T.wait(sw[m % 4], 16)
                        else:
                            T.wait(sv, CAST_T[m])
                for ih in range(2):
                    if ih == 0:
                        T.wait(sv, XNT_RDY[tb + 1])
                    for bank in (0, 1, 2, 3):
                        t_wait_bank(bank)
                    FILLS = []
                    for ic in (2 * ih, 2 * ih + 1):
                        bank = ic % 2
                        for c in range(8):
                            T.op("matmul", pb[bank][:, :],
                                 wslot((3 * e) % 5, c, ic, "gu"),
                                 xnT_c(c, off, TB),
                                 start=(c == 0), stop=(c == 7))
                        for c in range(8):
                            kk = T.op("matmul", pb[2 + bank][:, :],
                                      wslot((3 * e + 1) % 5, c, ic, "gu"),
                                      xnT_c(c, off, TB),
                                      start=(c == 0), stop=(c == 7))
                        FILLS.append(kk.inc(spe, 1))
                    FILL = FILLS[-1]
                    if tb == 3 and ih == 1:
                        RETIRE[3 * e] = FILL
                        RETIRE[3 * e + 1] = FILL
                    if k > 0:
                        SC.wait(sv, prev_inter[k - 1])
                    SILUS = []
                    for ii, ic in enumerate((2 * ih, 2 * ih + 1)):
                        SC.wait(spe, FILLS[ii])
                        kk = SC.op("activation",
                                   silu_sb[:, ic * TB:(ic + 1) * TB],
                                   pb[ic % 2][:, :], AF.Silu)
                        SILUS.append(kk.inc(ssc, 1))
                    SILU = SILUS[-1]
                    lastsc[0] = lastsc[1] = SILU
                    for ii, ic in enumerate((2 * ih, 2 * ih + 1)):
                        V.wait(ssc, SILUS[ii])
                        kk = V.op("tensor_tensor",
                                  t1_sb[:, ic * TB:(ic + 1) * TB],
                                  silu_sb[:, ic * TB:(ic + 1) * TB],
                                  pb[2 + ic % 2][:, :], ALU.mult)
                    T1 = kk.inc(sv, 1)
                    lastv[0] = lastv[1] = lastv[2] = lastv[3] = T1
                    if k > 1:
                        down_group(k - 2, ih)
                if e > 0:
                    V.wait(sv, MB_T)
                    V.drain()
                    for ic in range(4):
                        kk = V.op("tensor_tensor",
                                  ibuf[:, ic * TB:(ic + 1) * TB],
                                  t1_sb[:, ic * TB:(ic + 1) * TB],
                                  mbuf[:, :], ALU.mult)
                    prev_inter[k] = kk.inc(sv, 1)
                else:
                    V.drain()
                    prev_inter[k] = V.op("tensor_copy", ibuf[:, :],
                                         t1_sb[:, :]).inc(sv, 1)
                if tb == 1:
                    m = 3 * (e + 1)
                elif tb == 2:
                    m = 3 * (e + 1) + 1
                elif tb == 3:
                    m = 3 * (e + 1) + 2
                else:
                    m = -1
                if 4 <= m < 3 * (E + 1):
                    wstage_dma(m)

            emit_tile_front(0)
            emit_tile_mid(0)
            emit_tile_front(1)
            emit_tile_back(0)
            emit_tile_mid(1)
            emit_tile_front(2)
            emit_tile_back(1)
            emit_tile_mid(2)
            emit_tile_front(3)
            emit_tile_back(2)
            emit_tile_mid(3)
            emit_tile_front(4)
            emit_tile_back(3)
            emit_tile_mid(4)
            emit_tile_back(4)
            emit_tile_post(0)
            emit_step(0)
            emit_tile_post(1)
            emit_step(1)
            emit_tile_post(2)
            emit_step(2)
            emit_tile_post(3)
            for k in range(3, len(steps)):
                emit_step(k)
            dsrc = dmat.rearrange("(c p) d -> p c d", p=128)
            down_group(len(steps) - 2, 0)
            down_group(len(steps) - 2, 1)
            wstage("D0", RETIRE[3 * E - 2], src=dsrc[:, 0:4, :], kind="d", slot=1)
            down_group(len(steps) - 1, 0)
            down_group(len(steps) - 1, 1)
            wstage("D1", RETIRE[3 * E - 1], src=dsrc[:, 4:8, :], kind="d", slot=2)

            out_bufs = [out_buf, inter2_sb[:, :].bitcast(F32)]
            g_bufs = [silu_sb[:, 0:2 * TB], silu_sb[:, 2 * TB:4 * TB]]
            out_uses = [0, 0]
            g_last = [0, 0]
            prev_out = 0
            for tb in range(4):
                off = HALO + tb * TB
                for pr in range(4):
                    jj = tb * 4 + pr
                    par = jj % 2
                    obuf = out_bufs[par]
                    gbuf = g_bufs[par]
                    banks = (4, 5) if pr % 2 == 0 else (6, 7)
                    T.wait(sv, ACC_T[(E, tb, pr)])
                    if tb == 0 and pr == 0:
                        T.wait(sv, CAST_T["D1"])
                    for bank in banks:
                        t_wait_bank(bank)
                    for oc in (2 * pr, 2 * pr + 1):
                        bank = banks[oc % 2]
                        T.op("matmul", pb[bank][:, :],
                             csb[:, oc * 128:(oc + 1) * 128],
                             hs[:, off:off + TB], start=True, stop=False)
                        for c in range(8):
                            kk = T.op("matmul", pb[bank][:, :], dview(c, oc),
                                      xnT_c(c, off, TB),
                                      start=False, stop=(c == 7))
                    SMM = kk.inc(spe, 1)
                    SC.wait(sv, ACC_T[(E, tb, pr)])
                    if tb == 0 and pr == 0:
                        SC.wait(sv, prev_inter[len(steps) - 1])
                    if g_last[par]:
                        SC.wait(sv, g_last[par])
                    for oc in (2 * pr, 2 * pr + 1):
                        kk = SC.op("activation", gbuf[:, (oc % 2) * TB:(oc % 2 + 1) * TB],
                                   acc_v(oc, tb), AF.Silu)
                    GT = kk.inc(ssc, 1)
                    V.wait(spe, SMM)
                    V.wait(ssc, GT)
                    if out_uses[par]:
                        V.wait(so_sems[par], 16 * out_uses[par])
                    for oc in (2 * pr, 2 * pr + 1):
                        bank = banks[oc % 2]
                        kk = V.op("tensor_tensor", obuf[:, (oc % 2) * TB:(oc % 2 + 1) * TB],
                                  gbuf[:, (oc % 2) * TB:(oc % 2 + 1) * TB],
                                  pb[bank][:, :], ALU.mult)
                    OT = kk.inc(sv, 1)
                    prev_out = OT
                    g_last[par] = OT
                    for bank in banks:
                        lastv[bank] = OT
                    G.wait(sv, OT)
                    if tb == 0 and pr == 0:
                        G.wait(sx, 16 * n_sx)
                    G.op("dma_start",
                         out=out.rearrange("(c p) t -> p c t", p=128)[
                             :, 2 * pr:2 * pr + 2, tb * TB:(tb + 1) * TB],
                         in_=obuf[:, :].rearrange("p (c t) -> p c t", c=2),
                         accum_op=ALU.add).inc(so_sems[par], 16)
                    out_uses[par] += 1

            SY.wait(so_sems[0], 16 * out_uses[0])
            SY.wait(so_sems[1], 16 * out_uses[1])

        for name in ["sync", "gpsimd", "tensor", "scalar", "vector"]:
            deco = getattr(block, name)

            def mk(nm):
                def fn(api):
                    emit({nm: api})
                return fn

            deco(mk(name))

    return nc


_NC_CACHE = {}


def kernel(**inputs):
    x = np.ascontiguousarray(np.asarray(inputs["L0_attn_x"], dtype=np.float32))
    xf = x.reshape(NCORES * T_OWN, D)

    ident = np.eye(128, dtype=np.float32)
    selm = np.zeros((E, E * 128), np.float32)
    for e in range(E):
        selm[e, e * 128:(e + 1) * 128] = 1.0
    ones = np.ones((129, 1), np.float32)

    common = {
        "adiag": np.asarray(inputs["L0_attn_A_diag"], np.float32),
        "bmat": np.asarray(inputs["L0_attn_B"], np.float32),
        "cmat": np.asarray(inputs["L0_attn_C"], np.float32),
        "dmat": np.asarray(inputs["L0_attn_D"], np.float32),
        "gw": np.asarray(inputs["L0_ffn_gate_w"], np.float32),
        "gb": np.asarray(inputs["L0_ffn_gate_b"], np.float32),
        "swg": np.asarray(inputs["L0_ffn_shared_wg"], np.float32),
        "swu": np.asarray(inputs["L0_ffn_shared_wu"], np.float32),
        "swd": np.asarray(inputs["L0_ffn_shared_wd"], np.float32),
        "ewg": np.asarray(inputs["L0_ffn_experts_wg"], np.float32),
        "ewu": np.asarray(inputs["L0_ffn_experts_wu"], np.float32),
        "ewd": np.asarray(inputs["L0_ffn_experts_wd"], np.float32),
        "ident_in": ident,
        "selm_in": selm,
        "ones_in": ones[:128],
        "onesr_in": np.ones((1, 128), np.float32),
    }

    in_maps = []
    for c in range(NCORES):
        lo = c * T_OWN
        halo = (np.zeros((HALO, D), np.float32) if c % 4 == 0
                else xf[lo - HALO:lo])
        xt_c = np.ascontiguousarray(
            np.concatenate([halo, xf[lo:lo + T_OWN]], axis=0).T)
        im = dict(common)
        im["xt"] = xt_c
        in_maps.append(im)

    if "nc" not in _NC_CACHE:
        _NC_CACHE["nc"] = build_nc()
    nc = _NC_CACHE["nc"]

    res = run_bass_kernel_spmd(nc, in_maps, core_ids=list(range(NCORES)))
    outs = [np.asarray(r["out"]) for r in res.results]
    full = np.concatenate([o.T for o in outs], axis=0)
    return np.ascontiguousarray(full.reshape(2, NCORES * T_OWN // 2, D),
                                dtype=np.float32)


# version 30 (final)
# speedup vs baseline: 1.0201x; 1.0201x over previous
import numpy as np
from contextlib import ExitStack

import concourse.bass as bass
import concourse.mybir as mybir
from concourse.bass_utils import run_bass_kernel_spmd

F32 = mybir.dt.float32
BF16 = mybir.dt.bfloat16
F32R = mybir.dt.float32r
AF = mybir.ActivationFunctionType
ALU = mybir.AluOpType
AX = mybir.AxisListType

NCORES = 8
T_OWN = 2048
HALO = 256
T_TOT = T_OWN + HALO
D = 1024
I = 512
E = 16
DS = 64
TB = 512
EPS = 1e-5

A_TILES = [(0, HALO)] + [(HALO + t * TB, TB) for t in range(4)]


class Sem:

    def __init__(self, handle):
        self.h = handle
        self.n = 0


class Inst:
    __slots__ = ("raw",)

    def __init__(self, raw):
        self.raw = raw

    def inc(self, sem, k=1):
        sem.n += k
        if self.raw is not None:
            self.raw.then_inc(sem.h, k)
        return sem.n


class Eng:

    def __init__(self, real):
        self.real = real
        self.hwm = {}

    def op(self, name, *a, **kw):
        if self.real is None:
            return Inst(None)
        return Inst(getattr(self.real, name)(*a, **kw))

    def wait(self, sem, val):
        key = id(sem.h)
        if self.hwm.get(key, -1) >= val:
            return
        self.hwm[key] = val
        if self.real is not None:
            self.real.wait_ge(sem.h, val)

    def drain(self):
        if self.real is not None:
            self.real.drain()


def build_nc():
    nc = bass.Bass("TRN2", target_bir_lowering=False)

    xt = nc.dram_tensor("xt", [D, T_TOT], F32, kind="ExternalInput").ap()
    adiag = nc.dram_tensor("adiag", [DS], F32, kind="ExternalInput").ap()
    bmat = nc.dram_tensor("bmat", [D, DS], F32, kind="ExternalInput").ap()
    cmat = nc.dram_tensor("cmat", [DS, D], F32, kind="ExternalInput").ap()
    dmat = nc.dram_tensor("dmat", [D, D], F32, kind="ExternalInput").ap()
    gw = nc.dram_tensor("gw", [D, E], F32, kind="ExternalInput").ap()
    gb = nc.dram_tensor("gb", [E], F32, kind="ExternalInput").ap()
    swg = nc.dram_tensor("swg", [D, I], F32, kind="ExternalInput").ap()
    swu = nc.dram_tensor("swu", [D, I], F32, kind="ExternalInput").ap()
    swd = nc.dram_tensor("swd", [I, D], F32, kind="ExternalInput").ap()
    ewg = nc.dram_tensor("ewg", [E, D, I], F32, kind="ExternalInput").ap()
    ewu = nc.dram_tensor("ewu", [E, D, I], F32, kind="ExternalInput").ap()
    ewd = nc.dram_tensor("ewd", [E, I, D], F32, kind="ExternalInput").ap()
    ident_in = nc.dram_tensor("ident_in", [128, 128], F32, kind="ExternalInput").ap()
    selm_in = nc.dram_tensor("selm_in", [E, E * 128], F32, kind="ExternalInput").ap()
    ones_in = nc.dram_tensor("ones_in", [128, 1], F32, kind="ExternalInput").ap()
    onesr_in = nc.dram_tensor("onesr_in", [1, 128], F32, kind="ExternalInput").ap()
    out = nc.dram_tensor("out", [D, T_OWN], F32, kind="ExternalOutput").ap()

    es = ExitStack()
    with es:
        sb = lambda n, s, d: es.enter_context(nc.sbuf_tensor(n, s, d))
        ps = lambda n, s: es.enter_context(nc.psum_tensor(n, s, F32))

        xt_buf = sb("xt_buf", [128, 8 * TB], F32)
        acc = sb("acc", [128, 8 * T_OWN], F32)
        xnT = sb("xnT", [128, 8 * T_TOT], BF16)
        warena = sb("warena", [128, 5 * 4096], BF16)
        csb = sb("csb", [DS, D], BF16)
        bsb = sb("bsb", [128, 8 * DS], BF16)
        gsb = sb("gsb", [128, 8 * E], BF16)
        gb_sb = sb("gb_sb", [E, 1], F32)
        m_em = sb("m_em", [E, T_OWN], BF16)
        hs = sb("hs", [DS, T_TOT], BF16)
        rA = sb("rA", [1, TB], F32)
        rB = sb("rB", [1, TB], F32)
        rC = sb("rC", [1, TB], F32)
        rD = sb("rD", [1, TB], F32)
        rDb = rD[:, :].bitcast(BF16)
        rCb = rDb[0:1, 0:TB]
        rAb = rDb[0:1, TB:2 * TB]
        x2 = sb("x2", [128, 4 * TB], BF16)
        xb = sb("xb", [128, 2 * TB], BF16)
        rtb4 = sb("rtb4", [128, 64], F32)
        mk1 = sb("mk1", [128, 64], F32)
        rr2 = sb("rr2", [128, 64], F32)
        mk2 = sb("mk2", [128, 64], F32)
        m_tm = sb("m_tm", [128, 64], F32)
        v1t = [sb(f"v1t{b}", [128, 1], F32) for b in range(4)]
        v2t = [sb(f"v2t{b}", [128, 1], F32) for b in range(4)]
        dent = [sb(f"dent{b}", [128, 1], F32) for b in range(4)]
        rdent = [sb(f"rdent{b}", [128, 1], F32) for b in range(4)]
        sg1 = [sb(f"sg1{b}", [128, 1], F32) for b in range(4)]
        sg2 = [sb(f"sg2{b}", [128, 1], F32) for b in range(4)]
        a_sb = sb("a_sb", [DS, 1], F32)
        at1 = sb("at1", [DS, 1], F32)
        at2 = sb("at2", [DS, 1], F32)
        sig_a = sb("sig_a", [DS, 1], F32)
        silu_sb = sb("silu_sb", [128, 4 * TB], BF16)
        t1_sb = sb("t1_sb", [128, 4 * TB], BF16)
        inter_sb = sb("inter_sb", [128, 4 * TB], BF16)
        inter2_sb = sb("inter2_sb", [128, 4 * TB], BF16)
        m_sb = sb("m_sb", [128, TB], BF16)
        m2_sb = sb("m2_sb", [128, TB], BF16)
        g_sb = silu_sb[:, 0:2 * TB]
        out_buf = inter_sb[:, :].bitcast(F32)
        x2fv = x2[:, :].bitcast(F32)
        r_sb = x2fv[0:E, 0:TB]
        r2_sb = x2fv[0:E, TB:2 * TB]
        ident = sb("ident", [128, 128], F32)
        selm = sb("selm", [E, E * 128], BF16)
        onesc = sb("onesc", [128, 1], BF16)
        onescf = sb("onescf", [128, 1], F32)
        onesr = sb("onesr", [1, 128], F32)
        onesrb = sb("onesrb", [1, 128], BF16)

        pb = [ps(f"pb{i}", [128, 512]) for i in range(8)]

        sem_names = ["s_x", "s_x2", "s_xb", "s_xb2", "s_o", "s_o1", "s_gc",
                     "s_w0", "s_w1", "s_w2", "s_w3",
                     "s_pe", "s_v", "s_sc", "s_gp"]
        handles = {n: es.enter_context(nc.semaphore(n)) for n in sem_names}

        block = es.enter_context(nc.Block())

        def xt_c(c, w=TB):
            return xt_buf[:, c * TB:c * TB + w]

        def acc_v(c, tb, w=TB, off=0):
            return acc[:, c * T_OWN + tb * TB + off:c * T_OWN + tb * TB + off + w]

        def xnT_c(c, off, w):
            return xnT[:, c * T_TOT + off:c * T_TOT + off + w]

        def wslot(slot, k, oc, kind):
            base = slot * 4096
            if kind == "gu":
                o = base + k * 512 + oc * 128
            else:
                o = base + k * 1024 + oc * 128
            return warena[:, o:o + 128]

        def dview(k, oc):
            o = (1 + k // 4) * 4096 + (k % 4) * 1024 + oc * 128
            return warena[:, o:o + 128]

        def emit(active):
            S = {n: Sem(handles[n]) for n in sem_names}
            sx, sx2, sgc = S["s_x"], S["s_x2"], S["s_gc"]
            sxb, sxb2 = S["s_xb"], S["s_xb2"]
            so_sems = [S["s_o"], S["s_o1"]]
            sw = [S["s_w0"], S["s_w1"], S["s_w2"], S["s_w3"]]
            spe, sv, ssc, sgp = S["s_pe"], S["s_v"], S["s_sc"], S["s_gp"]

            EN = {n: Eng(active.get(n)) for n in
                  ["sync", "gpsimd", "tensor", "scalar", "vector"]}
            SY, G, T, SC, V = (EN["sync"], EN["gpsimd"], EN["tensor"],
                               EN["scalar"], EN["vector"])

            lastv = [0] * 8
            lastsc = [0] * 8

            def t_wait_bank(i):
                if lastv[i]:
                    T.wait(sv, lastv[i])
                if lastsc[i]:
                    T.wait(ssc, lastsc[i])

            SY.op("dma_start", out=ident[:, :], in_=ident_in).inc(sx, 16)
            SY.op("dma_start", out=onesr[:, :], in_=onesr_in).inc(sx, 16)
            SY.op("dma_start", out=onescf[:, :], in_=ones_in).inc(sx, 16)
            SY.op("dma_start", out=out, in_=xt[:, HALO:T_TOT]).inc(sx, 16)
            n_sx = 4

            G.op("dma_start", out=selm[:, :], in_=selm_in).inc(sgc, 16)
            G.op("dma_start", out=onesc[:, :], in_=ones_in).inc(sgc, 16)
            G.op("dma_start", out=onesrb[:, :], in_=onesr_in).inc(sgc, 16)
            G.op("dma_start", out=a_sb[:, :], in_=adiag[:, None]).inc(sgc, 16)
            G.op("dma_start", out=gb_sb[:, :], in_=gb[:, None]).inc(sgc, 16)
            G.op("dma_start", out=bsb[:, :].rearrange("p (c s) -> p c s", c=8),
                 in_=bmat.rearrange("(c p) s -> p c s", p=128)).inc(sgc, 16)
            G.op("dma_start", out=csb[:, :], in_=cmat).inc(sgc, 16)
            G.op("dma_start", out=gsb[:, :].rearrange("p (c e) -> p c e", c=8),
                 in_=gw.rearrange("(c p) e -> p c e", p=128)).inc(sgc, 16)
            n_gc = 8

            RETIRE = {}

            def w_src(m):
                e, j = divmod(m, 3)
                if e == 0:
                    hb = [swg, swu, swd][j]
                else:
                    hb = [ewg[e - 1], ewu[e - 1], ewd[e - 1]][j]
                if j < 2:
                    return hb.rearrange("(c p) i -> p c i", p=128), "gu"
                return hb.rearrange("(c p) d -> p c d", p=128), "d"

            slot_uses = [0, 0, 0, 0]

            SC.wait(sgc, 16 * n_gc)
            k = SC.op("activation", at1[:, :], a_sb[:, :], AF.Exp, scale=-1.0).inc(ssc, 1)
            V.wait(ssc, k)
            V.op("tensor_scalar_add", at2[:, :], at1[:, :], 1.0)
            V.drain()
            SIG_DONE = V.op("reciprocal", sig_a[:, :], at2[:, :]).inc(sv, 1)

            def gdma(m, wait_tkt):
                if wait_tkt:
                    G.wait(spe, wait_tkt)
                src, kind = w_src(m)
                slot = m % 4
                if kind == "gu":
                    dst = warena[:, slot * 4096:(slot + 1) * 4096].rearrange(
                        "p (c i) -> p c i", c=8)
                else:
                    dst = warena[:, slot * 4096:(slot + 1) * 4096].rearrange(
                        "p (c d) -> p c d", c=4)
                G.op("dma_start", out=dst, in_=src).inc(sw[slot], 16)
                slot_uses[slot] += 1

            for m in range(4):
                gdma(m, None)

            CAST_T = {}
            cast_last = [0]
            PEND = []

            def wstage_dma(m):
                nonlocal n_sx
                src, kind = w_src(m)
                nch = 8 if kind == "gu" else 4
                if cast_last[0]:
                    SY.wait(sv, cast_last[0])
                else:
                    SY.wait(sv, PHASE_A_V)
                SY.op("dma_start",
                      out=xt_buf[:, :].rearrange("p (c t) -> p c t", c=nch),
                      in_=src).inc(sx, 16)
                n_sx += 1
                PEND.append((m, kind, m % 5, 16 * n_sx))

            def wstage_cast():
                m, kind, slot, sxt = PEND.pop(0)
                nch, wch = (8, 512) if kind == "gu" else (4, 1024)
                V.wait(sx, sxt)
                r = RETIRE.get(m - 5)
                if r:
                    V.wait(spe, r)
                for c in range(nch):
                    kk = V.op("tensor_copy",
                              warena[:, slot * 4096 + c * wch:
                                     slot * 4096 + (c + 1) * wch],
                              xt_buf[:, c * wch:(c + 1) * wch])
                CAST_T[m] = kk.inc(sv, 1)
                cast_last[0] = CAST_T[m]

            def wstage(m, retire_tkt, src=None, kind=None, slot=None):
                nonlocal n_sx
                if src is None:
                    src, kind = w_src(m)
                    slot = m % 5
                nch, wch = (8, 512) if kind == "gu" else (4, 1024)
                if cast_last[0]:
                    SY.wait(sv, cast_last[0])
                else:
                    SY.wait(sv, PHASE_A_V)
                SY.op("dma_start",
                      out=xt_buf[:, :].rearrange("p (c t) -> p c t", c=nch),
                      in_=src).inc(sx, 16)
                n_sx += 1
                V.wait(sx, 16 * n_sx)
                if retire_tkt:
                    V.wait(spe, retire_tkt)
                for c in range(nch):
                    kk = V.op("tensor_copy",
                              warena[:, slot * 4096 + c * wch:
                                     slot * 4096 + (c + 1) * wch],
                              xt_buf[:, c * wch:(c + 1) * wch])
                CAST_T[m] = kk.inc(sv, 1)
                cast_last[0] = CAST_T[m]

            XNT_RDY = [0] * 5
            XT_FREE = [0] * 5
            XT_FREE_G = [0] * 5
            SCAN_T = [0] * 5
            M_RDY = [0] * 4
            stats_mm_t = {}
            n_sx2 = [0]
            n_sxb = [0]
            n_sxb2 = [0]
            prev_gacc = 0
            prev_rt_consumed = 0
            prev_topk = 0
            prev_mevac = 0
            prev_inter = {}
            ACC_T = {}
            steps = [(e, tb) for e in range(E + 1) for tb in range(4)]
            inter_bufs = [inter_sb, inter2_sb]
            m_bufs = [m_sb, m2_sb]
            PHASE_A_V = None

            EXP_T = {}
            xt2_f = [silu_sb[:, :].bitcast(F32), t1_sb[:, :].bitcast(F32),
                     inter_sb[:, :].bitcast(F32), inter2_sb[:, :].bitcast(F32)]

            def xt_v(it, c, w=TB):
                if it % 2 == 0:
                    return xt_buf[:, c * TB:c * TB + w]
                return xt2_f[c // 2][:, (c % 2) * TB:(c % 2) * TB + w]

            def emit_tile_front(it):
                nonlocal n_sx
                off, W = A_TILES[it]
                sb0, sb1 = (4, 5) if it % 2 == 0 else (0, 1)
                if it >= 2:
                    SY.wait(sv, XT_FREE[it - 2])
                xsrc = xt.rearrange("(c p) t -> p c t", p=128)
                if it % 2 == 0:
                    SY.op("dma_start",
                          out=xt_buf[:, :].rearrange("p (c t) -> p c t", c=8)[:, 0:4, 0:W],
                          in_=xsrc[:, 0:4, off:off + W]).inc(sx, 16)
                    n_sx += 1
                    SY.op("dma_start",
                          out=xt_buf[:, :].rearrange("p (c t) -> p c t", c=8)[:, 4:8, 0:W],
                          in_=xsrc[:, 4:8, off:off + W]).inc(sx2, 16)
                    n_sx2[0] += 1
                else:
                    for q in range(4):
                        dst = xt2_f[q].rearrange("p (c t) -> p c t", c=2)[:, :, 0:W]
                        ins = SY.op("dma_start", out=dst,
                                    in_=xsrc[:, 2 * q:2 * q + 2, off:off + W])
                        ins.inc(sxb if q < 2 else sxb2, 16)
                    n_sxb[0] += 2
                    n_sxb2[0] += 2
                c0s, c0n = (sx, 16 * n_sx) if it % 2 == 0 else (sxb, 16 * n_sxb[0])
                c4s, c4n = (sx2, 16 * n_sx2[0]) if it % 2 == 0 else (sxb2, 16 * n_sxb2[0])
                for c in range(8):
                    if c == 0:
                        V.wait(c0s, c0n)
                        SC.wait(c0s, c0n)
                    if c == 4:
                        V.wait(c4s, c4n)
                        SC.wait(c4s, c4n)
                    if it >= 1 and c < 2:
                        V.wait(spe, stats_mm_t[(it - 1, c + 6)])
                        SC.wait(spe, stats_mm_t[(it - 1, c + 6)])
                    if c >= 2:
                        V.wait(spe, stats_mm_t[(it, c - 2)])
                        SC.wait(spe, stats_mm_t[(it, c - 2)])
                    kc = SC.op("activation", xb[:, (c % 2) * TB:(c % 2) * TB + W],
                               xt_v(it, c, W), AF.Copy).inc(ssc, 1)
                    k = V.op("tensor_tensor", x2[:, (c % 4) * TB:(c % 4) * TB + W],
                             xt_v(it, c, W), xt_v(it, c, W), ALU.mult).inc(sv, 1)
                    if c == 0:
                        if it == 0:
                            T.wait(sgc, 16 * n_gc)
                        T.wait(c0s, c0n)
                        t_wait_bank(sb0)
                        t_wait_bank(sb1)
                    T.wait(ssc, kc)
                    T.wait(sv, k)
                    T.op("matmul", pb[sb0][0:1, 0:W], onesc[:, :],
                         xb[:, (c % 2) * TB:(c % 2) * TB + W],
                         start=(c == 0), stop=(c == 7))
                    kk = T.op("matmul", pb[sb1][0:1, 0:W], onesc[:, :],
                              x2[:, (c % 4) * TB:(c % 4) * TB + W],
                              start=(c == 0), stop=(c == 7)).inc(spe, 1)
                    stats_mm_t[(it, c)] = kk

            def emit_tile_mid(it):
                off, W = A_TILES[it]
                sb0, sb1 = (4, 5) if it % 2 == 0 else (0, 1)
                V.wait(spe, stats_mm_t[(it, 7)])
                V.op("tensor_copy", rA[:, 0:W], pb[sb0][0:1, 0:W])
                k = V.op("tensor_single_scalar", rB[:, 0:W], pb[sb1][0:1, 0:W],
                         float(D) * EPS, ALU.add).inc(sv, 1)
                lastv[sb0] = lastv[sb1] = k
                V.drain()
                V.op("tensor_scalar_mul", rC[:, 0:W], rA[:, 0:W], 1.0 / D)
                V.drain()
                V.op("tensor_mul", rD[:, 0:W], rC[:, 0:W], rC[:, 0:W])
                V.drain()
                k = V.op("scalar_tensor_tensor", rA[:, 0:W], rB[:, 0:W], 1.0 / D,
                         rD[:, 0:W], ALU.mult, ALU.subtract).inc(sv, 1)
                SC.wait(sv, k)
                SC.op("activation", rB[:, 0:W], rA[:, 0:W], AF.Ln)
                SC.drain()
                EXP_T[it] = SC.op("activation", rA[:, 0:W], rB[:, 0:W], AF.Exp,
                                  scale=-0.5).inc(ssc, 1)

            def emit_tile_back(it):
                nonlocal PHASE_A_V
                off, W = A_TILES[it]
                sb0, sb1 = (4, 5) if it % 2 == 0 else (0, 1)
                own = it >= 1
                tb = it - 1
                V.wait(ssc, EXP_T[it])
                V.op("tensor_copy", rCb[:, 0:W], rC[:, 0:W])
                k = V.op("tensor_copy", rAb[:, 0:W], rA[:, 0:W]).inc(sv, 1)
                T.wait(sv, k)
                t_wait_bank(sb0)
                t_wait_bank(sb1)
                T.op("matmul", pb[sb0][:, 0:W], onesrb[:, :], rCb[:, 0:W],
                     start=True, stop=True)
                k = T.op("matmul", pb[sb1][:, 0:W], onesrb[:, :], rAb[:, 0:W],
                         start=True, stop=True).inc(spe, 1)

                V.wait(spe, k)
                if own:
                    for c in range(8):
                        kk = V.op("tensor_tensor", acc_v(c, tb), xt_v(it, c),
                                  pb[sb0][:, 0:TB], ALU.subtract)
                    XT_FREE[it] = kk.inc(sv, 1)
                    V.drain()
                    for c in range(8):
                        V.op("tensor_tensor", acc_v(c, tb), acc_v(c, tb),
                             pb[sb1][:, 0:TB], ALU.mult)
                    V.drain()
                    V.op("tensor_copy",
                         xnT[:, :].rearrange("p (c t) -> p c t", c=8)[:, :, off:off + TB],
                         acc[:, :].rearrange("p (c t) -> p c t", c=8)[:, :, tb * TB:(tb + 1) * TB])
                else:
                    V.wait(spe, stats_mm_t[(1, 7)])
                    for r0 in (0, 4):
                        for c in range(r0, r0 + 4):
                            kk = V.op("tensor_tensor", x2[:, (c - r0) * W:(c - r0 + 1) * W],
                                 xt_v(it, c, W), pb[sb0][:, 0:W], ALU.subtract)
                        if r0 == 4:
                            XT_FREE[it] = kk.inc(sv, 1)
                        V.drain()
                        for c in range(r0, r0 + 4):
                            V.op("tensor_tensor", xnT_c(c, 0, W),
                                 x2[:, (c - r0) * W:(c - r0 + 1) * W],
                                 pb[sb1][:, 0:W], ALU.mult)
                        V.drain()
                XNT_RDY[it] = V.op("engine_nop").inc(sv, 1)
                lastv[sb0] = lastv[sb1] = XNT_RDY[it]

                T.wait(sv, XNT_RDY[it])
                t_wait_bank(7)
                for c in range(8):
                    mm = T.op("matmul", pb[7][0:DS, 0:W],
                              bsb[:, c * DS:(c + 1) * DS],
                              xnT_c(c, off, W),
                              start=(c == 0), stop=(c == 7))
                k = mm.inc(spe, 1)
                V.wait(spe, k)
                V.wait(sv, SIG_DONE)
                V.drain()
                init = 0.0 if it == 0 else hs[:, off - 1:off]
                SCAN_T[it] = V.op("tensor_tensor_scan", hs[:, off:off + W],
                                  sig_a[:, :].broadcast_to((DS, W)),
                                  pb[7][0:DS, 0:W], init,
                                  ALU.mult, ALU.add).inc(sv, 1)
                lastv[7] = SCAN_T[it]
                if it == 4:
                    PHASE_A_V = XT_FREE[4]

            SIG_T = {}
            PTK_T = {}

            def emit_post_a(tb):
                it = tb + 1
                off, W = A_TILES[it]
                if True:
                    T.wait(sv, XNT_RDY[it])
                    t_wait_bank(6)
                    for c in range(8):
                        mm = T.op("matmul", pb[6][0:E, 0:TB],
                                  gsb[:, c * E:(c + 1) * E],
                                  xnT_c(c, off, TB),
                                  start=(c == 0), stop=(c == 7))
                    k = mm.inc(spe, 1)
                    V.wait(spe, k)
                    k = V.op("tensor_scalar_add", r_sb, pb[6][0:E, 0:TB],
                             gb_sb[:, 0:1]).inc(sv, 1)
                    lastv[6] = k
                    T.wait(sv, k)
                    t_wait_bank(6)
                    for b in range(4):
                        mm = T.op("transpose", pb[6][:, b * 16:(b + 1) * 16],
                                  r_sb[:, b * 128:(b + 1) * 128],
                                  ident[0:E, 0:E])
                    k = mm.inc(spe, 1)
                    V.wait(spe, k)
                    for b in range(4):
                        kk = V.op("tensor_copy", rtb4[:, b * 16:(b + 1) * 16],
                                  pb[6][:, b * 16:(b + 1) * 16])
                    k = kk.inc(sv, 1)
                    lastv[6] = k
                    V.drain()
                    for b in range(4):
                        V.op("reduce_max", v1t[b][:, :],
                             rtb4[:, b * 16:(b + 1) * 16], axis=AX.X)
                    V.drain()
                    for b in range(4):
                        V.op("tensor_tensor", mk1[:, b * 16:(b + 1) * 16],
                             rtb4[:, b * 16:(b + 1) * 16],
                             v1t[b][:, :].broadcast_to((128, 16)), ALU.is_ge)
                    V.drain()
                    for b in range(4):
                        V.op("scalar_tensor_tensor", rr2[:, b * 16:(b + 1) * 16],
                             mk1[:, b * 16:(b + 1) * 16], -2.0,
                             rtb4[:, b * 16:(b + 1) * 16], ALU.mult, ALU.add)
                    V.drain()
                    for b in range(4):
                        V.op("reduce_max", v2t[b][:, :],
                             rr2[:, b * 16:(b + 1) * 16], axis=AX.X)
                    V.drain()
                    for b in range(4):
                        V.op("tensor_tensor", mk2[:, b * 16:(b + 1) * 16],
                             rtb4[:, b * 16:(b + 1) * 16],
                             v2t[b][:, :].broadcast_to((128, 16)), ALU.is_ge)
                    k = V.op("engine_nop").inc(sv, 1)
                    SC.wait(sv, k)
                    for b in range(4):
                        SC.op("activation", sg1[b][:, :], v1t[b][:, :], AF.Exp,
                              scale=-1.0)
                        kk = SC.op("activation", sg2[b][:, :], v2t[b][:, :],
                                   AF.Exp, scale=-1.0)
                    SIG_T[tb] = kk.inc(ssc, 1)

            def emit_post_b(tb):
                nonlocal prev_topk
                if True:
                    V.wait(ssc, SIG_T[tb])
                    for b in range(4):
                        V.op("tensor_scalar_add", sg1[b][:, :], sg1[b][:, :], 1.0)
                        V.op("tensor_scalar_add", sg2[b][:, :], sg2[b][:, :], 1.0)
                    V.drain()
                    for b in range(4):
                        V.op("reciprocal", v1t[b][:, :], sg1[b][:, :])
                        V.op("reciprocal", v2t[b][:, :], sg2[b][:, :])
                    V.drain()
                    for b in range(4):
                        V.op("tensor_add", dent[b][:, :], v1t[b][:, :],
                             v2t[b][:, :])
                    V.drain()
                    for b in range(4):
                        V.op("reciprocal", rdent[b][:, :], dent[b][:, :])
                    V.drain()
                    for b in range(4):
                        V.op("tensor_mul", sg1[b][:, :], v1t[b][:, :],
                             rdent[b][:, :])
                        V.op("tensor_mul", sg2[b][:, :], v2t[b][:, :],
                             rdent[b][:, :])
                        V.op("tensor_sub", mk2[:, b * 16:(b + 1) * 16],
                             mk2[:, b * 16:(b + 1) * 16],
                             mk1[:, b * 16:(b + 1) * 16])
                    V.drain()
                    for b in range(4):
                        V.op("tensor_single_scalar", m_tm[:, b * 16:(b + 1) * 16],
                             mk1[:, b * 16:(b + 1) * 16], sg1[b][:, 0:1],
                             ALU.mult)
                        V.op("tensor_single_scalar", rr2[:, b * 16:(b + 1) * 16],
                             mk2[:, b * 16:(b + 1) * 16], sg2[b][:, 0:1],
                             ALU.mult)
                    V.drain()
                    k = V.op("tensor_add", m_tm[:, 0:64], m_tm[:, 0:64],
                             rr2[:, 0:64]).inc(sv, 1)
                    prev_topk = k
                    lastv[6] = k
                    T.wait(sv, k)
                    t_wait_bank(6)
                    for b in range(4):
                        mm = T.op("transpose", pb[6][0:E, b * 128:(b + 1) * 128],
                                  m_tm[:, b * 16:(b + 1) * 16],
                                  ident[:, :])
                    k = mm.inc(spe, 1)
                    V.wait(spe, k)
                    k = V.op("tensor_copy", m_em[:, tb * TB:(tb + 1) * TB],
                             pb[6][0:E, 0:TB]).inc(sv, 1)
                    M_RDY[tb] = k
                    lastv[6] = k

            def down_group(k2, half):
                e2, tb2 = steps[k2]
                ibuf2 = inter_bufs[k2 % 2]
                T.wait(sv, prev_inter[k2])
                if tb2 == 0 and half == 0:
                    m = 3 * e2 + 2
                    if m < 4:
                        T.wait(sw[m % 4], 16)
                    else:
                        T.wait(sv, CAST_T[m])
                for pr in (2 * half, 2 * half + 1):
                    banks = (4, 5) if pr % 2 == 0 else (6, 7)
                    for bank in banks:
                        t_wait_bank(bank)
                    for oc in (2 * pr, 2 * pr + 1):
                        bank = banks[oc % 2]
                        for kk4 in range(4):
                            kk = T.op("matmul", pb[bank][:, :],
                                      wslot((3 * e2 + 2) % 5, kk4, oc, "d"),
                                      ibuf2[:, kk4 * TB:(kk4 + 1) * TB],
                                      start=(kk4 == 0), stop=(kk4 == 3))
                    DOWN = kk.inc(spe, 1)
                    if tb2 == 3 and pr == 3:
                        RETIRE[3 * e2 + 2] = DOWN
                    V.wait(spe, DOWN)
                    for oc in (2 * pr, 2 * pr + 1):
                        bank = banks[oc % 2]
                        kk = V.op("tensor_tensor", acc_v(oc, tb2),
                                  acc_v(oc, tb2), pb[bank][:, :], ALU.add)
                    AT = kk.inc(sv, 1)
                    ACC_T[(e2, tb2, pr)] = AT
                    for bank in banks:
                        lastv[bank] = AT

            def emit_step(k):
                e, tb = steps[k]
                off = HALO + tb * TB
                mbuf = m_bufs[k % 2]
                ibuf = inter_bufs[k % 2]
                if PEND:
                    wstage_cast()
                if e > 0:
                    t_wait_bank(4)
                    T.wait(sv, M_RDY[tb])
                    kk = T.op("matmul", pb[4][:, :],
                              selm[:, (e - 1) * 128:e * 128],
                              m_em[:, tb * TB:(tb + 1) * TB],
                              start=True, stop=True).inc(spe, 1)
                    V.wait(spe, kk)
                    MB_T = V.op("tensor_copy", mbuf[:, :], pb[4][:, :]).inc(sv, 1)
                    lastv[4] = MB_T
                if tb == 0:
                    for j in range(2):
                        m = 3 * e + j
                        if m < 4:
                            T.wait(sw[m % 4], 16)
                        else:
                            T.wait(sv, CAST_T[m])
                for ih in range(2):
                    if ih == 0:
                        T.wait(sv, XNT_RDY[tb + 1])
                    for bank in (0, 1, 2, 3):
                        t_wait_bank(bank)
                    FILLS = []
                    for ic in (2 * ih, 2 * ih + 1):
                        bank = ic % 2
                        for c in range(8):
                            T.op("matmul", pb[bank][:, :],
                                 wslot((3 * e) % 5, c, ic, "gu"),
                                 xnT_c(c, off, TB),
                                 start=(c == 0), stop=(c == 7))
                        for c in range(8):
                            kk = T.op("matmul", pb[2 + bank][:, :],
                                      wslot((3 * e + 1) % 5, c, ic, "gu"),
                                      xnT_c(c, off, TB),
                                      start=(c == 0), stop=(c == 7))
                        FILLS.append(kk.inc(spe, 1))
                    FILL = FILLS[-1]
                    if tb == 3 and ih == 1:
                        RETIRE[3 * e] = FILL
                        RETIRE[3 * e + 1] = FILL
                    if k > 0:
                        SC.wait(sv, prev_inter[k - 1])
                    SILUS = []
                    for ii, ic in enumerate((2 * ih, 2 * ih + 1)):
                        SC.wait(spe, FILLS[ii])
                        kk = SC.op("activation",
                                   silu_sb[:, ic * TB:(ic + 1) * TB],
                                   pb[ic % 2][:, :], AF.Silu)
                        SILUS.append(kk.inc(ssc, 1))
                    SILU = SILUS[-1]
                    lastsc[0] = lastsc[1] = SILU
                    for ii, ic in enumerate((2 * ih, 2 * ih + 1)):
                        V.wait(ssc, SILUS[ii])
                        kk = V.op("tensor_tensor",
                                  t1_sb[:, ic * TB:(ic + 1) * TB],
                                  silu_sb[:, ic * TB:(ic + 1) * TB],
                                  pb[2 + ic % 2][:, :], ALU.mult)
                    T1 = kk.inc(sv, 1)
                    lastv[0] = lastv[1] = lastv[2] = lastv[3] = T1
                    if k > 1:
                        down_group(k - 2, ih)
                if e > 0:
                    V.wait(sv, MB_T)
                    V.drain()
                    for ic in range(4):
                        kk = V.op("tensor_tensor",
                                  ibuf[:, ic * TB:(ic + 1) * TB],
                                  t1_sb[:, ic * TB:(ic + 1) * TB],
                                  mbuf[:, :], ALU.mult)
                    prev_inter[k] = kk.inc(sv, 1)
                else:
                    V.drain()
                    prev_inter[k] = V.op("tensor_copy", ibuf[:, :],
                                         t1_sb[:, :]).inc(sv, 1)
                if tb == 1:
                    m = 3 * (e + 1)
                elif tb == 2:
                    m = 3 * (e + 1) + 1
                elif tb == 3:
                    m = 3 * (e + 1) + 2
                else:
                    m = -1
                if 4 <= m < 3 * (E + 1):
                    wstage_dma(m)

            emit_tile_front(0)
            emit_tile_mid(0)
            emit_tile_front(1)
            emit_tile_back(0)
            emit_tile_mid(1)
            emit_tile_front(2)
            emit_tile_back(1)
            emit_tile_mid(2)
            emit_tile_front(3)
            emit_tile_back(2)
            emit_tile_mid(3)
            emit_tile_front(4)
            emit_tile_back(3)
            emit_tile_mid(4)
            emit_tile_back(4)
            emit_step(0)
            emit_post_a(0)
            emit_step(1)
            emit_post_b(0)
            emit_post_a(1)
            emit_step(2)
            emit_post_b(1)
            emit_post_a(2)
            emit_step(3)
            emit_post_b(2)
            emit_post_a(3)
            emit_step(4)
            emit_post_b(3)
            for k in range(5, len(steps)):
                emit_step(k)
            dsrc = dmat.rearrange("(c p) d -> p c d", p=128)
            down_group(len(steps) - 2, 0)
            down_group(len(steps) - 2, 1)
            wstage("D0", RETIRE[3 * E - 2], src=dsrc[:, 0:4, :], kind="d", slot=1)
            down_group(len(steps) - 1, 0)
            down_group(len(steps) - 1, 1)
            wstage("D1", RETIRE[3 * E - 1], src=dsrc[:, 4:8, :], kind="d", slot=2)

            out_bufs = [out_buf, inter2_sb[:, :].bitcast(F32)]
            g_bufs = [silu_sb[:, 0:2 * TB], silu_sb[:, 2 * TB:4 * TB]]
            out_uses = [0, 0]
            g_last = [0, 0]
            prev_out = 0
            for tb in range(4):
                off = HALO + tb * TB
                for pr in range(4):
                    jj = tb * 4 + pr
                    par = jj % 2
                    obuf = out_bufs[par]
                    gbuf = g_bufs[par]
                    banks = (4, 5) if pr % 2 == 0 else (6, 7)
                    T.wait(sv, ACC_T[(E, tb, pr)])
                    if tb == 0 and pr == 0:
                        T.wait(sv, CAST_T["D1"])
                    for bank in banks:
                        t_wait_bank(bank)
                    for oc in (2 * pr, 2 * pr + 1):
                        bank = banks[oc % 2]
                        T.op("matmul", pb[bank][:, :],
                             csb[:, oc * 128:(oc + 1) * 128],
                             hs[:, off:off + TB], start=True, stop=False)
                        for c in range(8):
                            kk = T.op("matmul", pb[bank][:, :], dview(c, oc),
                                      xnT_c(c, off, TB),
                                      start=False, stop=(c == 7))
                    SMM = kk.inc(spe, 1)
                    SC.wait(sv, ACC_T[(E, tb, pr)])
                    if tb == 0 and pr == 0:
                        SC.wait(sv, prev_inter[len(steps) - 1])
                    if g_last[par]:
                        SC.wait(sv, g_last[par])
                    for oc in (2 * pr, 2 * pr + 1):
                        kk = SC.op("activation", gbuf[:, (oc % 2) * TB:(oc % 2 + 1) * TB],
                                   acc_v(oc, tb), AF.Silu)
                    GT = kk.inc(ssc, 1)
                    V.wait(spe, SMM)
                    V.wait(ssc, GT)
                    if out_uses[par]:
                        V.wait(so_sems[par], 16 * out_uses[par])
                    for oc in (2 * pr, 2 * pr + 1):
                        bank = banks[oc % 2]
                        kk = V.op("tensor_tensor", obuf[:, (oc % 2) * TB:(oc % 2 + 1) * TB],
                                  gbuf[:, (oc % 2) * TB:(oc % 2 + 1) * TB],
                                  pb[bank][:, :], ALU.mult)
                    OT = kk.inc(sv, 1)
                    prev_out = OT
                    g_last[par] = OT
                    for bank in banks:
                        lastv[bank] = OT
                    G.wait(sv, OT)
                    if tb == 0 and pr == 0:
                        G.wait(sx, 16 * n_sx)
                    G.op("dma_start",
                         out=out.rearrange("(c p) t -> p c t", p=128)[
                             :, 2 * pr:2 * pr + 2, tb * TB:(tb + 1) * TB],
                         in_=obuf[:, :].rearrange("p (c t) -> p c t", c=2),
                         accum_op=ALU.add).inc(so_sems[par], 16)
                    out_uses[par] += 1

            SY.wait(so_sems[0], 16 * out_uses[0])
            SY.wait(so_sems[1], 16 * out_uses[1])

        for name in ["sync", "gpsimd", "tensor", "scalar", "vector"]:
            deco = getattr(block, name)

            def mk(nm):
                def fn(api):
                    emit({nm: api})
                return fn

            deco(mk(name))

    return nc


_NC_CACHE = {}


def kernel(**inputs):
    x = np.ascontiguousarray(np.asarray(inputs["L0_attn_x"], dtype=np.float32))
    xf = x.reshape(NCORES * T_OWN, D)

    ident = np.eye(128, dtype=np.float32)
    selm = np.zeros((E, E * 128), np.float32)
    for e in range(E):
        selm[e, e * 128:(e + 1) * 128] = 1.0
    ones = np.ones((129, 1), np.float32)

    common = {
        "adiag": np.asarray(inputs["L0_attn_A_diag"], np.float32),
        "bmat": np.asarray(inputs["L0_attn_B"], np.float32),
        "cmat": np.asarray(inputs["L0_attn_C"], np.float32),
        "dmat": np.asarray(inputs["L0_attn_D"], np.float32),
        "gw": np.asarray(inputs["L0_ffn_gate_w"], np.float32),
        "gb": np.asarray(inputs["L0_ffn_gate_b"], np.float32),
        "swg": np.asarray(inputs["L0_ffn_shared_wg"], np.float32),
        "swu": np.asarray(inputs["L0_ffn_shared_wu"], np.float32),
        "swd": np.asarray(inputs["L0_ffn_shared_wd"], np.float32),
        "ewg": np.asarray(inputs["L0_ffn_experts_wg"], np.float32),
        "ewu": np.asarray(inputs["L0_ffn_experts_wu"], np.float32),
        "ewd": np.asarray(inputs["L0_ffn_experts_wd"], np.float32),
        "ident_in": ident,
        "selm_in": selm,
        "ones_in": ones[:128],
        "onesr_in": np.ones((1, 128), np.float32),
    }

    in_maps = []
    for c in range(NCORES):
        lo = c * T_OWN
        halo = (np.zeros((HALO, D), np.float32) if c % 4 == 0
                else xf[lo - HALO:lo])
        xt_c = np.ascontiguousarray(
            np.concatenate([halo, xf[lo:lo + T_OWN]], axis=0).T)
        im = dict(common)
        im["xt"] = xt_c
        in_maps.append(im)

    if "nc" not in _NC_CACHE:
        _NC_CACHE["nc"] = build_nc()
    nc = _NC_CACHE["nc"]

    res = run_bass_kernel_spmd(nc, in_maps, core_ids=list(range(NCORES)))
    outs = [np.asarray(r["out"]) for r in res.results]
    full = np.concatenate([o.T for o in outs], axis=0)
    return np.ascontiguousarray(full.reshape(2, NCORES * T_OWN // 2, D),
                                dtype=np.float32)
